# version 01 (final)
# baseline (speedup 1.0000x reference)
import contextlib
import numpy as np
import ml_dtypes
import concourse.bass as bass
import concourse.mybir as mybir
from concourse.bass_utils import run_bass_kernel_spmd

F32 = mybir.dt.float32
BF16 = mybir.dt.bfloat16
U8 = mybir.dt.uint8
ALU = mybir.AluOpType
AF = mybir.ActivationFunctionType

T = 2048
NTC = 16
D = 4096
DCH = 32
II = 11008
ICH = 86
WGU = 2048
WD = 5504
TH = 1024
NB = [4] * 21 + [2]
I0B = np.cumsum([0] + NB).tolist()


def bcast_ap(view, ngrp, rep):
    return bass.AP(tensor=view.tensor, offset=view.offset,
                   ap=[list(view.ap[0]), [1, ngrp], [0, rep]])




def emit_ln_stats(v, xap, acc, acc2, sq, mvt, veps, eps=1e-5, inc=None):
    v.tensor_tensor(acc[:], xap[:, 0:512], xap[:, 512:1024], ALU.add)
    for c in range(2, 8):
        v.tensor_tensor(acc[:], acc[:], xap[:, c * 512:(c + 1) * 512], ALU.add)
    v.tensor_tensor(sq[:], xap[:, 0:512], xap[:, 0:512], ALU.mult)
    v.tensor_tensor(acc2[:], xap[:, 512:1024], xap[:, 512:1024], ALU.mult)
    v.tensor_tensor(acc2[:], acc2[:], sq[:], ALU.add)
    for c in range(2, 8):
        v.tensor_tensor(sq[:], xap[:, c * 512:(c + 1) * 512], xap[:, c * 512:(c + 1) * 512], ALU.mult)
        v.tensor_tensor(acc2[:], acc2[:], sq[:], ALU.add)
    w = 256
    while w >= 32:
        v.tensor_tensor(acc[:, 0:w], acc[:, 0:w], acc[:, w:2 * w], ALU.add)
        v.tensor_tensor(acc2[:, 0:w], acc2[:, 0:w], acc2[:, w:2 * w], ALU.add)
        w //= 2
    w = 16
    while w >= 1:
        rep = 32 // w
        for t in (acc, acc2):
            base = t[:, 0:1]
            pst = list(base.ap[0])
            outv = bass.AP(tensor=base.tensor, offset=base.offset, ap=[pst, [w, rep], [1, w]])
            in0 = bass.AP(tensor=base.tensor, offset=base.offset, ap=[pst, [0, rep], [1, w]])
            in1 = bass.AP(tensor=base.tensor, offset=base.offset + w, ap=[pst, [0, rep], [1, w]])
            v.tensor_tensor(outv, in0, in1, ALU.add)
        w //= 2
    def b32(view):
        return bass.AP(tensor=view.tensor, offset=view.offset, ap=[list(view.ap[0]), [0, 32]])
    v.tensor_scalar(mvt[:, 0:32], b32(acc[:, 0:1]), 1.0 / 4096, None, ALU.mult)
    v.tensor_scalar(acc2[:, 0:32], b32(acc2[:, 0:1]), 1.0 / 4096, None, ALU.mult)
    v.tensor_tensor(acc[:, 0:32], mvt[:, 0:32], mvt[:, 0:32], ALU.mult)
    v.tensor_tensor(mvt[:, 32:64], acc2[:, 0:32], acc[:, 0:32], ALU.subtract)
    ins = v.tensor_scalar(veps[:], mvt[:, 32:64], eps, None, ALU.add)
    if inc is not None:
        ins.then_inc(*inc)


def bc8(view):
    return bass.AP(tensor=view.tensor, offset=view.offset,
                   ap=[list(view.ap[0]), [0, 8]])


DBG_G = False
DBG4 = False
STG_LATE = False


def build_program(debug_stage=4):
    nc = bass.Bass("TRN2", target_bir_lowering=False, detect_race_conditions=False)

    x_in = nc.dram_tensor("x_in", [T, D], F32, kind="ExternalInput").ap()
    wqT = nc.dram_tensor("wqT", [D, D], F32, kind="ExternalInput").ap()
    wkT = nc.dram_tensor("wkT", [D, 1024], F32, kind="ExternalInput").ap()
    wvT = nc.dram_tensor("wvT", [D, 1024], F32, kind="ExternalInput").ap()
    woT = nc.dram_tensor("woT", [D, D], F32, kind="ExternalInput").ap()
    bq_pp = nc.dram_tensor("bq_pp", [128, 32], F32, kind="ExternalInput").ap()
    bkv_b = nc.dram_tensor("bkv_b", [1, 2048], F32, kind="ExternalInput").ap()
    bo_b = nc.dram_tensor("bo_b", [1, D], F32, kind="ExternalInput").ap()
    gq_u8 = nc.dram_tensor("gq_u8", [II, WGU], U8, kind="ExternalInput").ap()
    uq_u8 = nc.dram_tensor("uq_u8", [II, WGU], U8, kind="ExternalInput").ap()
    dq_u8 = nc.dram_tensor("dq_u8", [D, WD], U8, kind="ExternalInput").ap()
    g_sc = nc.dram_tensor("g_sc", [II, 64], F32, kind="ExternalInput").ap()
    g_zp = nc.dram_tensor("g_zp", [II, 64], F32, kind="ExternalInput").ap()
    u_sc = nc.dram_tensor("u_sc", [II, 64], F32, kind="ExternalInput").ap()
    u_zp = nc.dram_tensor("u_zp", [II, 64], F32, kind="ExternalInput").ap()
    d_sc = nc.dram_tensor("d_sc", [D, 172], F32, kind="ExternalInput").ap()
    d_zp = nc.dram_tensor("d_zp", [D, 172], F32, kind="ExternalInput").ap()
    out = nc.dram_tensor("out", [T, D], F32, kind="ExternalOutput").ap()

    xn_scr = nc.dram_tensor("xn_scr", [T, D], BF16).ap()
    x1_scr = nc.dram_tensor("x1_scr", [T, D], F32).ap()
    x2_scr = nc.dram_tensor("x2_scr", [T, D], F32).ap()
    natg = nc.dram_tensor("natg", [II, D], BF16).ap()
    natu = nc.dram_tensor("natu", [II, D], BF16).ap()
    natd = nc.dram_tensor("natd", [D, II], BF16).ap()
    wdt_scr = nc.dram_tensor("wdt_scr", [16, 128, ICH, 256], BF16).ap()
    h_scr = nc.dram_tensor("h_scr", [ICH, 128, T], BF16).ap()
    y_scr = nc.dram_tensor("y_scr", [T, D], BF16).ap()
    q_scr = nc.dram_tensor("q_scr", [32, 128, T], BF16).ap()
    cc_src_t = nc.dram_tensor("cc_src", [128, 1032], F32)
    cc_dst_t = nc.dram_tensor("cc_dst", [128, 1032], F32)

    dq_seq = []
    for bidx, nb in enumerate(NB):
        for j in range(nb):
            dq_seq.append(("g", I0B[bidx] + j))
        for j in range(nb):
            dq_seq.append(("u", I0B[bidx] + j))
    for ob in range(32):
        for wn in range(4):
            dq_seq.append(("d", ob * 4 + wn))
    NDQ = len(dq_seq)
    natinc = [16 if m_ != "d" else 32 for (m_, i_) in dq_seq]
    natcum = np.cumsum([0] + natinc).tolist()
    natpos = {}
    scz_cum = []
    npairs = 0
    for k, (m_, i_) in enumerate(dq_seq):
        natpos[(m_, i_)] = k + 1
        if m_ != "d" or i_ % 4 == 0:
            npairs += 1
        scz_cum.append(npairs)

    win_seq = []
    for half in range(2):
        for bidx in range(len(NB)):
            win_seq.append(("g", half, bidx))
            win_seq.append(("u", half, bidx))
    NWIN = len(win_seq)

    gu_groups = []
    win_end = {}
    ng = nu = 0
    for half in range(2):
        for bidx, nb in enumerate(NB):
            for mat in ("g", "u"):
                wseq = half * 44 + 2 * bidx + (0 if mat == "g" else 1)
                for l in range(nb):
                    for tb in range(2):
                        if mat == "g":
                            ng += 1
                            n = ng
                        else:
                            nu += 1
                            n = nu
                        gu_groups.append((half, bidx, mat, l, tb, wseq, n))
                win_end[wseq] = (mat, ng if mat == "g" else nu)
    NGG = ng

    ctx = contextlib.ExitStack()
    with ctx:
        sb = lambda name, shape, dt: ctx.enter_context(nc.sbuf_tensor(name, shape, dt))
        S = lambda name: ctx.enter_context(nc.semaphore(name))
        pbank = [ctx.enter_context(nc.psum_tensor(f"pb{i}", [128, 512], F32)) for i in range(8)]

        ones_r = sb("ones_r", [1, 128], BF16)
        ones_c = sb("ones_c", [128, 8], BF16)
        sP0 = S("sP0")
        sBkv = S("sBkv")
        sBq = S("sBq")

        sXin = [S(f"sXin{f}") for f in range(2)]
        sRstS = [S(f"sRstS{f}") for f in range(2)]
        sXnW = [S(f"sXnW{i}") for i in range(3)]
        sXnT = [S(f"sXnT{f}") for f in range(2)]
        sWin = [S(f"sWin{f}") for f in range(2)]
        sWinD = [S(f"sWinD{f}") for f in range(2)]
        sPEg = [S(f"sPEg{f}") for f in range(2)]
        sPEu = [S(f"sPEu{f}") for f in range(2)]
        sSilu = [S(f"sSilu{f}") for f in range(2)]
        sUm = [S(f"sUm{f}") for f in range(2)]
        sHW = [S(f"sHW{f}") for f in range(2)]
        sH = [S(f"sH{f}") for f in range(2)]
        sWd = [S(f"sWd{f}") for f in range(2)]
        sPEd = [S(f"sPEd{f}") for f in range(2)]
        sYE = [S(f"sYE{f}") for f in range(2)]
        sYW = [S(f"sYW{f}") for f in range(2)]
        sXr = [S(f"sXr{f}") for f in range(2)]
        sYr = [S(f"sYr{f}") for f in range(2)]
        sEpi = [S(f"sEpi{f}") for f in range(2)]
        sEpiR = [S(f"sEpiR{f}") for f in range(2)]
        sX1W = S("sX1W")
        sOutW = S("sOutW")
        sU8 = S("sU8")
        sSCZ = S("sSCZ")
        sDqA = S("sDqA")
        sDqB = S("sDqB")
        sNat = S("sNat")
        sStgR = S("sStgR")
        sStgW = S("sStgW")
        sWk = S("sWk")
        sPEk = S("sPEk")
        sKm = S("sKm")
        sKe = S("sKe")
        sEvK = S("sEvK")
        sKVp = S("sKVp")
        sKVe = S("sKVe")
        sCCs = S("sCCs")
        sCC = S("sCC")
        sXnT2b = S("sXnT2b")
        sWq = S("sWq")
        sPEq = S("sPEq")
        sQa = S("sQa")
        sQe = S("sQe")
        sEvQ = S("sEvQ")
        sQW = S("sQW")
        sKVl = S("sKVl")
        sKVb = S("sKVb")
        sQr = S("sQr")
        sPEa = S("sPEa")
        sEvA = S("sEvA")
        sWo = S("sWo")
        sX1s = S("sX1s")
        sPEo = S("sPEo")
        sEvO = S("sEvO")
        sX2W = S("sX2W")

        with nc.Block() as b:
            @b.vector
            def _(v):
                v.memset(ones_r[:], 1.0).then_inc(sP0, 1)
                v.memset(ones_c[:], 1.0).then_inc(sP0, 1)

        def emit_ffn_gu(f, x_src, xn_idx):
            ph = contextlib.ExitStack()
            with ph:
                psb = lambda n, shp, dt: ph.enter_context(nc.sbuf_tensor(f"{n}_{f}", shp, dt))
                xt = psb("xt", [128, D], F32)
                xnt = psb("xnt", [128, D], BF16)
                acc = psb("acc", [128, 512], F32)
                acc2 = psb("acc2", [128, 512], F32)
                sqb = psb("sqb", [128, 512], F32)
                mvt = psb("mvt", [128, 64], F32)
                rst = psb("rst", [128, 32], F32)
                lnt = psb("lnt", [128, 32], F32)
                veps = psb("veps", [128, 32], F32)
                xnT = psb("xnT", [128, DCH, TH], BF16)
                dum_t = psb("dum_t", [128, 128], BF16)
                wwin = psb("wwin", [128, 2, DCH, 512], BF16)
                sgt = psb("sgt", [128, 4, TH], BF16)
                hst = psb("hst", [128, 2, TH], BF16)
                if f == 0:
                    dq_w8 = psb("dq_w8", [128, 2, WGU], U8)
                    dq_hi8 = psb("dq_hi8", [128, WGU], U8)
                    dq_lo8 = psb("dq_lo8", [128, WGU], U8)
                    dq_hif = psb("dq_hif", [128, WGU // 2], F32)
                    dq_lof = psb("dq_lof", [128, WGU // 2], F32)
                    dq_sc = psb("dq_sc", [128, 2, 172], F32)
                    dq_zp = psb("dq_zp", [128, 2, 172], F32)
                    dq_zs = psb("dq_zs", [128, 172], F32)
                    dq_pl = psb("dq_pl", [128, 2, WGU], BF16)
                    stg_t = psb("stg_t", [128, D // 2], BF16)

                chains_ln = []
                c = 0
                for tc in range(NTC):
                    if tc % 2 == 1 and c < 8:
                        c += 1
                    chains_ln.append(c)
                chains_sb = []
                for half in range(2):
                    for bidx in range(len(NB)):
                        c = min(NDQ, c + 10)
                        chains_sb.append(c)
                chains_sb[-1] = NDQ
                dq_cursor = [0]

                def dq_subwindows(wj):
                    pieces = []
                    off = 0
                    while off < wj:
                        ln = min(1024, wj - off)
                        pieces.append((off, ln))
                        off += ln
                    return pieces

                def emit_dq_vector(v, k):
                    m_, i_ = dq_seq[k]
                    wj = WGU if m_ != "d" else 1376
                    ngrp = wj // 32
                    gof = 0 if m_ != "d" else (i_ % 4) * 43
                    scslot = (k % 2) if m_ != "d" else (i_ // 4) % 2
                    v.wait_ge(sU8, 32 * (k + 1))
                    v.wait_ge(sSCZ, 64 * scz_cum[k])
                    if k >= 1:
                        v.wait_ge(sNat, natcum[k])
                    w8 = dq_w8[:, k % 2, 0:wj]
                    v.tensor_scalar(dq_hi8[:, 0:wj], w8, 4, None, ALU.logical_shift_right)
                    v.tensor_scalar(dq_lo8[:, 0:wj], w8, 15, None, ALU.bitwise_and).then_inc(sDqA, 1)
                    scv = dq_sc[:, scslot, gof:gof + ngrp]
                    zpv = dq_zp[:, scslot, gof:gof + ngrp]
                    v.tensor_tensor(dq_zs[:, 0:ngrp], zpv, scv, ALU.mult).then_inc(sDqA, 1)
                    subs = dq_subwindows(wj)
                    for si, (off, ln) in enumerate(subs):
                        gr0, grn = off // 32, ln // 32
                        v.tensor_copy(dq_hif[:, 0:ln], dq_hi8[:, off:off + ln])
                        hi3 = dq_hif[:, 0:ln].rearrange("p (g j) -> p g j", j=32)
                        pl3 = dq_pl[:, 0, off:off + ln].rearrange("p (g j) -> p g j", j=32)
                        v.tensor_tensor(hi3, hi3, bcast_ap(scv[:, gr0:gr0 + grn], grn, 32), ALU.mult)
                        v.tensor_tensor(pl3, hi3, bcast_ap(dq_zs[:, gr0:gr0 + grn], grn, 32), ALU.subtract)
                        if si == len(subs) - 1:
                            gl = gr0 + grn - 1
                            v.tensor_tensor(dq_pl[:, 0, off + ln - 32:off + ln].rearrange("p (g j) -> p g j", j=32),
                                            dq_hif[:, ln - 32:ln].rearrange("p (g j) -> p g j", j=32),
                                            bcast_ap(dq_zs[:, gl:gl + 1], 1, 32),
                                            ALU.subtract).then_inc(sDqA, 1)

                def emit_dq_pool(g, k):
                    m_, i_ = dq_seq[k]
                    wj = WGU if m_ != "d" else 1376
                    gof = 0 if m_ != "d" else (i_ % 4) * 43
                    scslot = (k % 2) if m_ != "d" else (i_ // 4) % 2
                    scv = dq_sc[:, scslot, gof:gof + wj // 32]
                    g.wait_ge(sDqA, 3 * k + 2)
                    if k >= 1:
                        g.wait_ge(sNat, natcum[k])
                    subs = dq_subwindows(wj)
                    for si, (off, ln) in enumerate(subs):
                        gr0, grn = off // 32, ln // 32
                        g.tensor_copy(dq_lof[:, 0:ln], dq_lo8[:, off:off + ln])
                        lo3 = dq_lof[:, 0:ln].rearrange("p (g j) -> p g j", j=32)
                        pl3 = dq_pl[:, 1, off:off + ln].rearrange("p (g j) -> p g j", j=32)
                        g.tensor_tensor(lo3, lo3, bcast_ap(scv[:, gr0:gr0 + grn], grn, 32), ALU.mult)
                        g.tensor_tensor(pl3, lo3, bcast_ap(dq_zs[:, gr0:gr0 + grn], grn, 32), ALU.subtract)
                        if si == len(subs) - 1:
                            gl = gr0 + grn - 1
                            g.tensor_tensor(dq_pl[:, 1, off + ln - 32:off + ln].rearrange("p (g j) -> p g j", j=32),
                                            dq_lof[:, ln - 32:ln].rearrange("p (g j) -> p g j", j=32),
                                            bcast_ap(dq_zs[:, gl:gl + 1], 1, 32),
                                            ALU.subtract).then_inc(sDqB, 1)

                with nc.Block() as b:
                    @b.sync
                    def _(sy):
                        for tc in range(NTC):
                            if tc >= 1:
                                sy.wait_ge(sXnW[xn_idx], 16 * tc)
                            if f == 1 and tc == 0:
                                sy.wait_ge(sX2W, 16 * 128)
                            sy.dma_start(out=xt[:], in_=x_src[tc * 128:(tc + 1) * 128, :]).then_inc(sXin[f], 16)
                            sy.dma_start(out=xt[:], in_=x_src[tc * 128:(tc + 1) * 128, :]).then_inc(sXin[f], 16)
                        stg_rd = [0]

                        def emit_stage_reads(n):
                            for _ in range(n):
                                i = stg_rd[0]
                                if i >= 2 * ICH:
                                    return
                                ic, hv = i // 2, i % 2
                                if i == 0:
                                    sy.wait_ge(sNat, natcum[NDQ])
                                if i >= 1:
                                    sy.wait_ge(sStgW, 128 * i)
                                sy.dma_start(out=stg_t[:],
                                             in_=natd[hv * 2048:(hv + 1) * 2048, ic * 128:(ic + 1) * 128],
                                             transpose=True).then_inc(sStgR, 16)
                                stg_rd[0] += 1

                        for half in range(2):
                            sy.wait_ge(sXnW[xn_idx], 16 * 8 * (half + 1))
                            if half == 1:
                                sy.wait_ge(sPEu[f], NGG // 2)
                            for rep in range(2):
                                for ch in range(DCH):
                                    src = xn_scr[half * TH:(half + 1) * TH, ch * 128:(ch + 1) * 128]
                                    sy.dma_start(out=xnT[:, ch, :], in_=src, transpose=True).then_inc(sXnT[f], 16)
                            for wl in range(44):
                                wseq = half * 44 + wl
                                mat, _, bidx = win_seq[wseq]
                                nb = NB[bidx]
                                if wseq >= 2:
                                    pm, pcnt = win_end[wseq - 2]
                                    sy.wait_ge(sPEg[f] if pm == "g" else sPEu[f], pcnt)
                                if f == 0:
                                    need = max(natpos[(mat, I0B[bidx] + j)] for j in range(nb))
                                    sy.wait_ge(sNat, natcum[need])
                                natm = natg if mat == "g" else natu
                                for ch in range(DCH):
                                    src = natm[I0B[bidx] * 128:(I0B[bidx] + nb) * 128, ch * 128:(ch + 1) * 128]
                                    sy.dma_start(out=wwin[:, wseq % 2, ch, 0:nb * 128], in_=src,
                                                 transpose=True).then_inc(sWin[f], 16)
                                if f == 0 and half == 1 and bidx >= 9:
                                    emit_stage_reads(7)
                            lw = win_seq[half * 44 + 43]
                            lnb = NB[lw[2]]
                            for ch in range(DCH):
                                src = (natg if lw[0] == "g" else natu)[I0B[lw[2]] * 128:(I0B[lw[2]] + lnb) * 128,
                                                                       ch * 128:(ch + 1) * 128]
                                sy.dma_start(out=wwin[:, 43 % 2, ch, 0:lnb * 128], in_=src,
                                             transpose=True).then_inc(sWin[f], 16)
                        if f == 0:
                            emit_stage_reads(2 * ICH)

                    if f == 0:
                        @b.gpsimd
                        def _(g):
                            emitted_loads = [0]

                            def load_block(k):
                                m_, i_ = dq_seq[k]
                                if k >= 2:
                                    g.wait_ge(sDqB, k - 1)
                                for rep in range(2):
                                    if m_ == "g":
                                        g.dma_start(out=dq_w8[:, k % 2, :], in_=gq_u8[i_ * 128:(i_ + 1) * 128, :]).then_inc(sU8, 16)
                                        g.dma_start(out=dq_sc[:, k % 2, 0:64], in_=g_sc[i_ * 128:(i_ + 1) * 128, :]).then_inc(sSCZ, 16)
                                        g.dma_start(out=dq_zp[:, k % 2, 0:64], in_=g_zp[i_ * 128:(i_ + 1) * 128, :]).then_inc(sSCZ, 16)
                                    elif m_ == "u":
                                        g.dma_start(out=dq_w8[:, k % 2, :], in_=uq_u8[i_ * 128:(i_ + 1) * 128, :]).then_inc(sU8, 16)
                                        g.dma_start(out=dq_sc[:, k % 2, 0:64], in_=u_sc[i_ * 128:(i_ + 1) * 128, :]).then_inc(sSCZ, 16)
                                        g.dma_start(out=dq_zp[:, k % 2, 0:64], in_=u_zp[i_ * 128:(i_ + 1) * 128, :]).then_inc(sSCZ, 16)
                                    else:
                                        ob, wn = i_ // 4, i_ % 4
                                        g.dma_start(out=dq_w8[:, k % 2, 0:1376],
                                                    in_=dq_u8[ob * 128:(ob + 1) * 128, wn * 1376:(wn + 1) * 1376]).then_inc(sU8, 16)
                                        if wn == 0:
                                            g.dma_start(out=dq_sc[:, ob % 2, :], in_=d_sc[ob * 128:(ob + 1) * 128, :]).then_inc(sSCZ, 16)
                                            g.dma_start(out=dq_zp[:, ob % 2, :], in_=d_zp[ob * 128:(ob + 1) * 128, :]).then_inc(sSCZ, 16)

                            load_block(0)
                            for k in range(NDQ):
                                if k + 1 < NDQ:
                                    load_block(k + 1)
                                emit_dq_pool(g, k)

                    @b.vector
                    def _(v):
                        for tc in range(NTC):
                            v.wait_ge(sXin[f], 32 * (tc + 1))
                            emit_ln_stats(v, xt, acc, acc2, sqb, mvt, veps,
                                          inc=(sRstS[f], 1))
                            v.wait_ge(sRstS[f], 4 * tc + 3)
                            rv = rst[:, 0:1]
                            if xn_idx != 1:
                                xv = xt[:]
                                srcv = bass.AP(tensor=xv.tensor, offset=xv.offset,
                                               ap=[list(xv.ap[0]), [1, 2], [2, 2048]])
                                dstv = xnt[:].rearrange("p (par j) -> p par j", par=2)
                                rb = bass.AP(tensor=rv.tensor, offset=rv.offset,
                                             ap=[list(rv.ap[0]), [0, 2], [0, 2048]])
                            else:
                                srcv = xt[:]
                                dstv = xnt[:]
                                rb = bass.AP(tensor=rv.tensor, offset=rv.offset,
                                             ap=[list(rv.ap[0]), [0, D]])
                            v.scalar_tensor_tensor(dstv, srcv, mvt[:, 0:1], rb,
                                                   ALU.subtract, ALU.mult)
                            rb32 = bass.AP(tensor=rv.tensor, offset=rv.offset,
                                           ap=[list(rv.ap[0]), [0, 32]])
                            if xn_idx != 1:
                                tsrc = bass.AP(tensor=xt[:].tensor, offset=xt[:].offset + 4096 - 63,
                                               ap=[list(xt[:].ap[0]), [2, 32]])
                            else:
                                tsrc = xt[:, D - 32:D]
                            v.scalar_tensor_tensor(xnt[:, D - 32:D], tsrc, mvt[:, 0:1], rb32,
                                                   ALU.subtract, ALU.mult).then_inc(sRstS[f], 1)
                            if f == 0:
                                while dq_cursor[0] < chains_ln[tc]:
                                    emit_dq_vector(v, dq_cursor[0]); dq_cursor[0] += 1
                        for idx, (half, bidx, mat, l, tb, wseq, n) in enumerate(gu_groups):
                            if mat != "u":
                                continue
                            if f == 0 and l == 0 and tb == 0:
                                sbi = half * 22 + bidx
                                while dq_cursor[0] < chains_sb[sbi]:
                                    emit_dq_vector(v, dq_cursor[0]); dq_cursor[0] += 1
                            v.wait_ge(sPEu[f], n)
                            v.wait_ge(sSilu[f], n)
                            ic = I0B[bidx] + l
                            hw_idx = half * ICH + ic
                            if hw_idx >= 2 and tb == 0:
                                v.wait_ge(sHW[f], 16 * (hw_idx - 1))
                            bank = pbank[2 + (2 * l + tb) % 2]
                            if DBG_G:
                                v.tensor_copy(hst[:, hw_idx % 2, tb * 512:(tb + 1) * 512],
                                              sgt[:, l, tb * 512:(tb + 1) * 512])
                                v.engine_nop().then_inc(sUm[f], 1)
                            else:
                                v.tensor_tensor(hst[:, hw_idx % 2, tb * 512:(tb + 1) * 512],
                                                bank[:, 0:512], sgt[:, l, tb * 512:(tb + 1) * 512],
                                                ALU.mult)
                                v.tensor_tensor(hst[:, hw_idx % 2, tb * 512 + 480:(tb + 1) * 512],
                                                bank[:, 480:512], sgt[:, l, tb * 512 + 480:(tb + 1) * 512],
                                                ALU.mult).then_inc(sUm[f], 1)

                        if f == 0:
                            while dq_cursor[0] < NDQ:
                                emit_dq_vector(v, dq_cursor[0]); dq_cursor[0] += 1

                    @b.scalar
                    def _(s):
                        nat_done = [0]

                        def emit_nat_writes(upto):
                            while nat_done[0] < min(upto, NDQ):
                                k = nat_done[0]
                                m_, i_ = dq_seq[k]
                                s.wait_ge(sDqA, 3 * k + 3)
                                s.wait_ge(sDqB, k + 1)
                                if m_ in ("g", "u"):
                                    natm = natg if m_ == "g" else natu
                                    s.dma_start(out=natm[i_ * 128:(i_ + 1) * 128, :].rearrange("p (pl j) -> p pl j", pl=2),
                                                in_=dq_pl[:, :, :]).then_inc(sNat, 16)
                                else:
                                    ob, wn = i_ // 4, i_ % 4
                                    s.dma_start(out=natd[ob * 128:(ob + 1) * 128, wn * 1376:(wn + 1) * 1376],
                                                in_=dq_pl[:, 0, 0:1376]).then_inc(sNat, 16)
                                    s.dma_start(out=natd[ob * 128:(ob + 1) * 128, WD + wn * 1376:WD + (wn + 1) * 1376],
                                                in_=dq_pl[:, 1, 0:1376]).then_inc(sNat, 16)
                                nat_done[0] += 1

                        stg_done = [0]

                        def emit_stage_b(n_batches):
                            for _ in range(n_batches):
                                i = stg_done[0]
                                if i >= 2 * ICH:
                                    return
                                ic, hv = i // 2, i % 2
                                s.wait_ge(sStgR, 16 * (i + 1))
                                for sl in range(8):
                                    s.dma_start(out=wdt_scr[hv * 8 + sl, :, ic, :],
                                                in_=stg_t[:, sl * 256:(sl + 1) * 256]).then_inc(sStgW, 16)
                                stg_done[0] += 1

                        for tc in range(NTC):
                            s.wait_ge(sRstS[f], 4 * tc + 1)
                            s.activation(lnt[:], veps[:], AF.Ln)
                            s.activation(rst[:], lnt[:], AF.Exp, scale=-0.5).then_inc(sRstS[f], 2)
                            s.wait_ge(sRstS[f], 4 * tc + 4)
                            s.dma_start(out=xn_scr[tc * 128:(tc + 1) * 128, :], in_=xnt[:]).then_inc(sXnW[xn_idx], 16)
                            if f == 0:
                                emit_nat_writes(chains_ln[tc])
                        hw_count = [0]
                        for idx, (half, bidx, mat, l, tb, wseq, n) in enumerate(gu_groups):
                            if mat == "g":
                                if f == 0 and l == 0 and tb == 0:
                                    sbi = half * 22 + bidx
                                    emit_nat_writes(chains_sb[sbi])
                                s.wait_ge(sPEg[f], n)
                                if n > 8:
                                    s.wait_ge(sUm[f], n - 8)
                                bank = pbank[(2 * l + tb) % 2]
                                s.activation(sgt[:, l, tb * 512:(tb + 1) * 512], bank[:, 0:512],
                                             AF.Identity if DBG_G else AF.Silu)
                                s.activation(sgt[:, l, tb * 512 + 480:(tb + 1) * 512], bank[:, 480:512],
                                             AF.Identity if DBG_G else AF.Silu).then_inc(sSilu[f], 1)
                            else:
                                if tb == 1:
                                    ic = I0B[bidx] + l
                                    hw_idx = half * ICH + ic
                                    s.wait_ge(sUm[f], n)
                                    s.dma_start(out=h_scr[ic, :, half * TH:(half + 1) * TH],
                                                in_=hst[:, hw_idx % 2, :]).then_inc(sHW[f], 16)
                                    hw_count[0] += 1
                                    if f == 0 and l == NB[bidx] - 1:
                                        sbi = half * 22 + bidx
                                        emit_nat_writes(chains_sb[sbi])
                                        if half == 1 and bidx >= 8 and not STG_LATE:
                                            emit_stage_b(14)
                        if f == 0:
                            emit_nat_writes(NDQ)
                            if STG_LATE:
                                s.wait_ge(sPEu[f], NGG)
                            emit_stage_b(2 * ICH)

                    @b.tensor
                    def _(t):
                        cur_half = -1
                        cur_wseq = -1
                        for idx, (half, bidx, mat, l, tb, wseq, n) in enumerate(gu_groups):
                            if half != cur_half:
                                t.wait_ge(sXnT[f], 16 * 2 * DCH * (half + 1))
                                cur_half = half
                            if wseq != cur_wseq:
                                if wseq <= 42:
                                    wcnt = DCH * (wseq + 2)
                                elif wseq == 43:
                                    wcnt = DCH * 45
                                elif wseq <= 86:
                                    wcnt = DCH * (wseq + 3)
                                else:
                                    wcnt = DCH * 90
                                t.wait_ge(sWin[f], 16 * wcnt)
                                cur_wseq = wseq
                            if mat == "g":
                                bank = pbank[(2 * l + tb) % 2]
                                if n > 2:
                                    t.wait_ge(sSilu[f], n - 2)
                            else:
                                bank = pbank[2 + (2 * l + tb) % 2]
                                if n > 2:
                                    t.wait_ge(sUm[f], n - 2)
                            for ch in range(DCH):
                                ins = t.matmul(bank[:, 0:512],
                                               wwin[:, wseq % 2, ch, l * 128:(l + 1) * 128],
                                               xnT[:, ch, tb * 512:(tb + 1) * 512],
                                               start=(ch == 0), stop=(ch == DCH - 1))
                            ins.then_inc(sPEg[f] if mat == "g" else sPEu[f], 1)

        def emit_ffn_down(f):
            ph = contextlib.ExitStack()
            with ph:
                psb = lambda n, shp, dt: ph.enter_context(nc.sbuf_tensor(f"{n}_{f}", shp, dt))
                hsl = psb("hsl", [128, ICH, 512], BF16)
                wdp = psb("wdp", [128, 2, ICH, 256], BF16)
                ysl = psb("ysl", [128, 2, 256], BF16)

                with nc.Block() as b:
                    @b.sync
                    def _(sy):
                        sy.wait_ge(sHW[f], 16 * 2 * ICH)
                        if f == 0:
                            sy.wait_ge(sStgW, 16 * 16 * ICH)
                        for tt in range(4):
                            if tt >= 1:
                                sy.wait_ge(sPEd[f], 64 * tt)
                            for ic in range(ICH):
                                sy.dma_start(out=hsl[:, ic, :], in_=h_scr[ic, :, tt * 512:(tt + 1) * 512]).then_inc(sH[f], 16)
                            for os in range(16):
                                seq = tt * 16 + os
                                if seq >= 2:
                                    sy.wait_ge(sPEd[f], (seq - 2) * 4 + 4)
                                sy.dma_start(out=wdp[:, seq % 2, :, :], in_=wdt_scr[os, :, :, :]).then_inc(sWd[f], 16)
                            sy.dma_start(out=wdp[:, 15 % 2, :, :], in_=wdt_scr[15, :, :, :]).then_inc(sWd[f], 16)

                    @b.tensor
                    def _(t):
                        for tt in range(4):
                            t.wait_ge(sH[f], 16 * ICH * (tt + 1))
                            for os in range(16):
                                seq = tt * 16 + os
                                wcnt = (seq + 2 + tt) if os <= 14 else 17 * (tt + 1)
                                t.wait_ge(sWd[f], 16 * wcnt)
                                for tc in range(4):
                                    g = seq * 4 + tc
                                    bank = pbank[g % 2]
                                    if g >= 2:
                                        t.wait_ge(sYE[f], g - 1)
                                    for ic in range(ICH):
                                        ins = t.matmul(bank[:, 0:256],
                                                       hsl[:, ic, tc * 128:(tc + 1) * 128],
                                                       wdp[:, seq % 2, ic, :],
                                                       start=(ic == 0), stop=(ic == ICH - 1))
                                    ins.then_inc(sPEd[f], 1)

                    @b.vector
                    def _(v):
                        for g in range(256):
                            v.wait_ge(sPEd[f], g + 1)
                            if g >= 2:
                                v.wait_ge(sYW[f], 16 * (g - 1))
                            bank = pbank[g % 2]
                            v.tensor_scalar(ysl[:, g % 2, :], bank[:, 0:256], 0.5, None, ALU.mult)
                            v.tensor_scalar(ysl[:, g % 2, 224:256], bank[:, 224:256], 0.5, None,
                                            ALU.mult).then_inc(sYE[f], 1)

                    @b.scalar
                    def _(s):
                        for tt in range(4):
                            for os in range(16):
                                for tc in range(4):
                                    g = (tt * 16 + os) * 4 + tc
                                    s.wait_ge(sYE[f], g + 1)
                                    dst = y_scr[(tt * 4 + tc) * 128:(tt * 4 + tc + 1) * 128,
                                                os * 256:(os + 1) * 256]
                                    s.dma_start(out=dst, in_=ysl[:, g % 2, :]).then_inc(sYW[f], 16)

        def emit_epilogue(f, x_src, dst, xn_idx, wr_sem):
            ph = contextlib.ExitStack()
            with ph:
                psb = lambda n, shp, dt: ph.enter_context(nc.sbuf_tensor(f"{n}e_{f}", shp, dt))
                xt = psb("ext", [128, 2, D], F32)
                yt = psb("eyt", [128, 2, D], BF16)
                x1t = psb("ex1", [128, 2, D], F32)
                xnt = psb("exn", [128, 2, D], BF16) if xn_idx is not None else None
                acc = psb("eacc", [128, 512], F32)
                acc2 = psb("eac2", [128, 512], F32)
                sqb = psb("esqb", [128, 512], F32)
                mvt = psb("emv", [128, 64], F32)
                rst = psb("erst", [128, 32], F32)
                lnt = psb("elnt", [128, 32], F32)
                veps = psb("evep", [128, 32], F32)

                with nc.Block() as b:
                    @b.sync
                    def _(sy):
                        sy.wait_ge(sYW[f], 16 * 256)
                        for tc in range(NTC):
                            if tc >= 2:
                                sy.wait_ge(sEpi[f], 2 * (tc - 1))
                            sy.dma_start(out=xt[:, tc % 2, :], in_=x_src[tc * 128:(tc + 1) * 128, :]).then_inc(sXr[f], 16)
                            sy.dma_start(out=xt[:, tc % 2, :], in_=x_src[tc * 128:(tc + 1) * 128, :]).then_inc(sXr[f], 16)
                            sy.dma_start(out=yt[:, tc % 2, :], in_=y_scr[tc * 128:(tc + 1) * 128, :]).then_inc(sYr[f], 16)
                            sy.dma_start(out=yt[:, tc % 2, :], in_=y_scr[tc * 128:(tc + 1) * 128, :]).then_inc(sYr[f], 16)

                    @b.vector
                    def _(v):
                        for tc in range(NTC):
                            v.wait_ge(sXr[f], 32 * (tc + 1))
                            v.wait_ge(sYr[f], 32 * (tc + 1))
                            if tc >= 2:
                                v.wait_ge(wr_sem, 16 * (tc - 1))
                            v.tensor_tensor(x1t[:, tc % 2, :], xt[:, tc % 2, :], yt[:, tc % 2, :], ALU.add)
                            v.tensor_tensor(x1t[:, tc % 2, D - 32:D], xt[:, tc % 2, D - 32:D],
                                            yt[:, tc % 2, D - 32:D], ALU.add)
                            if xn_idx is not None:
                                emit_ln_stats(v, x1t[:, tc % 2, :], acc, acc2, sqb, mvt, veps,
                                              inc=(sEpiR[f], 1))
                                v.wait_ge(sEpiR[f], 3 * tc + 3)
                                if tc >= 2:
                                    v.wait_ge(sXnW[xn_idx], 16 * (tc - 1))
                                rv = rst[:, 0:1]
                                rb = bass.AP(tensor=rv.tensor, offset=rv.offset,
                                             ap=[list(rv.ap[0]), [0, D]])
                                v.scalar_tensor_tensor(xnt[:, tc % 2, :], x1t[:, tc % 2, :], mvt[:, 0:1], rb,
                                                       ALU.subtract, ALU.mult)
                                rb32 = bass.AP(tensor=rv.tensor, offset=rv.offset,
                                               ap=[list(rv.ap[0]), [0, 32]])
                                v.scalar_tensor_tensor(xnt[:, tc % 2, D - 32:D], x1t[:, tc % 2, D - 32:D],
                                                       mvt[:, 0:1], rb32, ALU.subtract, ALU.mult)
                            v.engine_nop().then_inc(sEpi[f], 2)

                    @b.scalar
                    def _(s):
                        for tc in range(NTC):
                            if xn_idx is not None:
                                s.wait_ge(sEpiR[f], 3 * tc + 1)
                                s.activation(lnt[:], veps[:], AF.Ln)
                                s.activation(rst[:], lnt[:], AF.Exp, scale=-0.5).then_inc(sEpiR[f], 2)
                            s.wait_ge(sEpi[f], 2 * (tc + 1))
                            s.dma_start(out=dst[tc * 128:(tc + 1) * 128, :], in_=x1t[:, tc % 2, :]).then_inc(wr_sem, 16)
                            if xn_idx is not None:
                                s.dma_start(out=xn_scr[tc * 128:(tc + 1) * 128, :], in_=xnt[:, tc % 2, :]).then_inc(sXnW[xn_idx], 16)
                        s.wait_ge(wr_sem, 16 * NTC)

        emit_ffn_gu(0, x_in, 0)
        if debug_stage >= 2:
            emit_ffn_down(0)
        if debug_stage == 1:
            with nc.Block() as b:
                @b.sync
                def _(sy):
                    sy.wait_ge(sXnW[0], 16 * NTC)

                @b.gpsimd
                def _(g):
                    g.wait_ge(sXnW[0], 16 * NTC)
                    g.dma_start(out=out[:, :], in_=xn_scr[:, :]).then_inc(sOutW, 16)
                    g.wait_ge(sOutW, 16)
            nc.finalize()
            return nc

        emit_epilogue(0, x_in, x1_scr if debug_stage >= 3 else out,
                      1 if debug_stage >= 3 else None,
                      sX1W if debug_stage >= 3 else sOutW)
        if debug_stage == 2:
            dbg_ng = nc.dram_tensor("dbg_ng", [256, D], F32, kind="ExternalOutput").ap()
            dbg_nu = nc.dram_tensor("dbg_nu", [256, D], F32, kind="ExternalOutput").ap()
            dbg_nd = nc.dram_tensor("dbg_nd", [256, II], F32, kind="ExternalOutput").ap()
            dbg_h = nc.dram_tensor("dbg_h", [ICH, 128, T], F32, kind="ExternalOutput").ap()
            dbg_y = nc.dram_tensor("dbg_y", [256, D], F32, kind="ExternalOutput").ap()
            dbg_w = nc.dram_tensor("dbg_w", [128, ICH, 256], F32, kind="ExternalOutput").ap()
            sDbg = S("sDbg")
            with nc.Block() as b:
                @b.gpsimd
                def _(g):
                    g.wait_ge(sOutW, 16 * NTC)
                    g.dma_start(out=dbg_ng[:, :], in_=natg[2048:2304, :]).then_inc(sDbg, 16)
                    g.dma_start(out=dbg_nu[:, :], in_=natu[2048:2304, :]).then_inc(sDbg, 16)
                    g.dma_start(out=dbg_nd[:, :], in_=natd[0:256, :]).then_inc(sDbg, 16)
                    g.dma_start(out=dbg_h[:, :, :], in_=h_scr[:, :, :]).then_inc(sDbg, 16)
                    g.dma_start(out=dbg_y[:, :], in_=y_scr[0:256, :]).then_inc(sDbg, 16)
                    g.dma_start(out=dbg_w[:, :, :], in_=wdt_scr[3, :, :, :]).then_inc(sDbg, 16)
                    g.wait_ge(sDbg, 96)
            nc.finalize()
            return nc

        ph2 = contextlib.ExitStack()
        with ph2:
            psb = lambda n, shp, dt: ph2.enter_context(nc.sbuf_tensor(n, shp, dt))
            xnTa = psb("xnTa", [128, DCH, TH], BF16)
            wkv = psb("wkv", [128, 2, DCH, 512], BF16)
            ktl = psb("ktl", [128, 8, 1024], BF16)
            vtl = psb("vtl", [128, 8, 1024], BF16)
            kev = psb("kev", [128, 512], F32)
            kev2 = psb("kev2", [128, 512], F32)
            kvacc = psb("kvacc", [128, 1024], F32)
            ksacc = psb("ksacc", [128, 8], F32)
            bkv_s = psb("bkv_s", [1, 2048], BF16)

            with nc.Block() as b:
                @b.sync
                def _(sy):
                    for half in range(2):
                        sy.wait_ge(sXnW[1], 16 * 8 * (half + 1))
                        if half == 1:
                            sy.wait_ge(sPEk, 32)
                        for rep in range(2):
                            for ch in range(DCH):
                                src = xn_scr[half * TH:(half + 1) * TH, ch * 128:(ch + 1) * 128]
                                sy.dma_start(out=xnTa[:, ch, :], in_=src, transpose=True).then_inc(sXnT2b, 16)

                @b.gpsimd
                def _(g):
                    g.dma_start(out=bkv_s[:], in_=bkv_b[:]).then_inc(sBkv, 16)
                    for w in range(8):
                        half, wl = w // 4, w % 4
                        mat = wkT if wl < 2 else wvT
                        os = wl % 2
                        if w >= 2:
                            g.wait_ge(sPEk, 8 * (w - 1))
                        for rep in range(2):
                            for piece in range(4):
                                src = mat[piece * 1024:(piece + 1) * 1024, os * 512:(os + 1) * 512]
                                g.dma_start(out=wkv[:, w % 2, piece * 8:(piece + 1) * 8, :],
                                            in_=src.rearrange("(c p) o -> p c o", p=128)).then_inc(sWk, 16)
                        g.dma_start(out=wkv[:, w % 2, 0, :],
                                    in_=mat[0:128, os * 512:(os + 1) * 512]).then_inc(sWk, 16)
                    g.wait_ge(sKVe, 16)
                    g.dma_start(out=cc_src_t[:, 0:1024], in_=kvacc[:]).then_inc(sCCs, 16)
                    g.dma_start(out=cc_src_t[:, 1024:1032], in_=ksacc[:]).then_inc(sCCs, 16)
                    g.wait_ge(sCCs, 32)
                    g.collective_compute(
                        "AllReduce", ALU.add,
                        replica_groups=[[0, 1], [2, 3], [4, 5], [6, 7]],
                        ins=[cc_src_t.ap().opt()], outs=[cc_dst_t.ap().opt()],
                    ).then_inc(sCC, 1)

                @b.tensor
                def _(t):
                    t.wait_ge(sP0, 2)
                    t.wait_ge(sBkv, 16)
                    for half in range(2):
                        t.wait_ge(sXnT2b, 16 * 2 * DCH * (half + 1))
                        for wl in range(4):
                            w = half * 4 + wl
                            os = wl % 2
                            is_v = wl >= 2
                            t.wait_ge(sWk, 16 * 9 * (w + 1))
                            for tcl in range(8):
                                q = half * 32 + wl * 8 + tcl
                                bank = pbank[q % 4]
                                if q >= 4:
                                    t.wait_ge(sEvK, q - 3)
                                for ch in range(DCH):
                                    t.matmul(bank[:, 0:512],
                                             xnTa[:, ch, tcl * 128:(tcl + 1) * 128],
                                             wkv[:, w % 2, ch, :],
                                             start=(ch == 0), stop=False)
                                boff = (1024 if is_v else 0) + os * 512
                                ins = t.matmul(bank[:, 0:512], ones_r[:, :],
                                               bkv_s[0:1, boff:boff + 512],
                                               start=False, stop=True)
                                ins.then_inc(sPEk, 1)
                        for tcl in range(8):
                            tcg = half * 8 + tcl
                            t.wait_ge(sEvK, half * 32 + 24 + tcl + 1)
                            if tcg >= 1:
                                t.wait_ge(sKVe, tcg)
                            for gh in range(8):
                                bank = pbank[4 + gh // 4]
                                t.matmul(bank[:, (gh % 4) * 128:(gh % 4 + 1) * 128],
                                         ktl[:, tcl, gh * 128:(gh + 1) * 128],
                                         vtl[:, tcl, gh * 128:(gh + 1) * 128],
                                         start=True, stop=True)
                            for gh in range(8):
                                ins = t.matmul(pbank[6][:, gh:gh + 1],
                                               ktl[:, tcl, gh * 128:(gh + 1) * 128],
                                               ones_c[:, 0:1], start=True, stop=True)
                            ins.then_inc(sKVp, 1)

                @b.vector
                def _(v):
                    v.memset(kvacc[:], 0.0)
                    v.memset(ksacc[:], 0.0)
                    for half in range(2):
                        for wl in range(4):
                            w = half * 4 + wl
                            os = wl % 2
                            is_v = wl >= 2
                            for tcl in range(8):
                                q = half * 32 + wl * 8 + tcl
                                bank = pbank[q % 4]
                                v.wait_ge(sPEk, q + 1)
                                if half == 1 and wl == 0 and tcl == 0:
                                    v.wait_ge(sKVp, 8)
                                base = (vtl if is_v else ktl)[:, tcl, os * 512:(os + 1) * 512]
                                tail = (vtl if is_v else ktl)[:, tcl, os * 512 + 480:(os + 1) * 512]
                                if not is_v:
                                    nk = half * 16 + os * 8 + tcl
                                    v.tensor_scalar(kev[:], bank[:, 0:512], 0.0, None, ALU.min).then_inc(sKm, 1)
                                    v.wait_ge(sKe, nk + 1)
                                    v.scalar_tensor_tensor(base, bank[:, 0:512], 0.0, kev2[:],
                                                           ALU.max, ALU.add)
                                    v.scalar_tensor_tensor(tail, bank[:, 480:512], 0.0, kev2[:, 480:512],
                                                           ALU.max, ALU.add).then_inc(sEvK, 1)
                                else:
                                    v.tensor_copy(base, bank[:, 0:512])
                                    v.tensor_copy(tail, bank[:, 480:512]).then_inc(sEvK, 1)
                        for tcl in range(8):
                            tcg = half * 8 + tcl
                            v.wait_ge(sKVp, tcg + 1)
                            v.tensor_tensor(kvacc[:, 0:512], kvacc[:, 0:512], pbank[4][:, :], ALU.add)
                            v.tensor_tensor(kvacc[:, 512:1024], kvacc[:, 512:1024], pbank[5][:, :], ALU.add)
                            v.tensor_tensor(ksacc[:], ksacc[:], pbank[6][:, 0:8], ALU.add)
                            if tcg == 15:
                                v.tensor_copy(kvacc[:], kvacc[:])
                                v.tensor_copy(kvacc[:], kvacc[:])
                            else:
                                v.tensor_copy(kvacc[:, 960:1024], kvacc[:, 960:1024])
                            v.tensor_copy(ksacc[:], ksacc[:])
                            v.tensor_copy(ksacc[:], ksacc[:]).then_inc(sKVe, 1)

                @b.scalar
                def _(s):
                    for half in range(2):
                        for osl in range(2):
                            for tcl in range(8):
                                n = half * 16 + osl * 8 + tcl
                                s.wait_ge(sKm, n + 1)
                                s.activation(kev2[:], kev[:], AF.Exp).then_inc(sKe, 1)

        ph3 = contextlib.ExitStack()
        with ph3:
            psb = lambda n, shp, dt: ph3.enter_context(nc.sbuf_tensor(n, shp, dt))
            xnTq = psb("xnTq", [128, DCH, T], BF16)
            wqw = psb("wqw", [128, 2, DCH, 256], BF16)
            bq_s = psb("bq_s", [128, 32], F32)
            qt0 = psb("qt0", [128, 512], F32)
            qt1 = psb("qt1", [128, 512], F32)
            qt2 = psb("qt2", [128, 512], F32)
            qtl = psb("qtl", [128, 2, 2048], BF16)

            with nc.Block() as b:
                @b.sync
                def _(sy):
                    sy.wait_ge(sXnW[1], 16 * NTC)
                    for rep in range(2):
                        for ch in range(DCH):
                            sy.dma_start(out=xnTq[:, ch, :], in_=xn_scr[:, ch * 128:(ch + 1) * 128],
                                         transpose=True).then_inc(sXnT2b, 16)

                @b.gpsimd
                def _(g):
                    g.dma_start(out=bq_s[:], in_=bq_pp[:]).then_inc(sBq, 16)
                    for w in range(16):
                        if w >= 2:
                            g.wait_ge(sPEq, 8 * (w - 1))
                        for rep in range(2):
                            for piece in range(4):
                                src = wqT[piece * 1024:(piece + 1) * 1024, w * 256:(w + 1) * 256]
                                g.dma_start(out=wqw[:, w % 2, piece * 8:(piece + 1) * 8, :],
                                            in_=src.rearrange("(c p) o -> p c o", p=128)).then_inc(sWq, 16)
                        g.dma_start(out=wqw[:, w % 2, 0, :],
                                    in_=wqT[0:128, w * 256:(w + 1) * 256]).then_inc(sWq, 16)

                @b.tensor
                def _(t):
                    t.wait_ge(sXnT2b, 16 * 2 * DCH * 2 + 16 * 2 * DCH)
                    for dch in range(32):
                        w = dch // 2
                        t.wait_ge(sWq, 16 * 9 * (w + 1))
                        for tb in range(4):
                            n = dch * 4 + tb
                            bank = pbank[n % 4]
                            if n >= 4:
                                t.wait_ge(sEvQ, n - 3)
                            for ch in range(DCH):
                                ins = t.matmul(bank[:, 0:512],
                                               wqw[:, w % 2, ch, (dch % 2) * 128:(dch % 2 + 1) * 128],
                                               xnTq[:, ch, tb * 512:(tb + 1) * 512],
                                               start=(ch == 0), stop=(ch == DCH - 1))
                            ins.then_inc(sPEq, 1)

                @b.vector
                def _(v):
                    v.wait_ge(sBq, 16)
                    for dch in range(32):
                        for tb in range(4):
                            n = dch * 4 + tb
                            bank = pbank[n % 4]
                            v.wait_ge(sPEq, n + 1)
                            bqv = bq_s[:, dch:dch + 1]
                            v.tensor_scalar(qt0[:], bank[:, 0:512], bqv, 0.0, ALU.add, ALU.max)
                            v.tensor_scalar(qt1[:], bank[:, 0:512], bqv, 0.0, ALU.add, ALU.min).then_inc(sQa, 1)
                            v.wait_ge(sQe, n + 1)
                            if dch >= 2 and tb == 0:
                                v.wait_ge(sQW, 16 * (dch - 1))
                            v.tensor_tensor(qtl[:, dch % 2, tb * 512:(tb + 1) * 512], qt0[:], qt2[:], ALU.add)
                            v.tensor_tensor(qtl[:, dch % 2, tb * 512 + 480:(tb + 1) * 512], qt0[:, 480:512],
                                            qt2[:, 480:512], ALU.add).then_inc(sEvQ, 1)

                @b.scalar
                def _(s):
                    for n in range(128):
                        s.wait_ge(sQa, n + 1)
                        s.activation(qt2[:], qt1[:], AF.Exp).then_inc(sQe, 1)
                        if (n + 1) % 4 == 0:
                            dch = n // 4
                            s.wait_ge(sEvQ, n + 1)
                            s.dma_start(out=q_scr[dch, :, :], in_=qtl[:, dch % 2, :]).then_inc(sQW, 16)

        if debug_stage == 3 or debug_stage == 4:
            x2dst = x2_scr if debug_stage == 4 else out
            x2sem = sX2W if debug_stage == 4 else sOutW
            if DBG4:
                dbg_kvr = nc.dram_tensor("dbg_kvr", [128, 1032], F32, kind="ExternalOutput").ap()
                dbg_atn = nc.dram_tensor("dbg_atn", [128, 32, T], BF16, kind="ExternalOutput").ap()
                sDbg3 = S("sDbg3")
            ph4 = contextlib.ExitStack()
            with ph4:
                psb = lambda n, shp, dt: ph4.enter_context(nc.sbuf_tensor(n, shp, dt))
                kvr = psb("kvr", [128, 1032], F32)
                kvb = psb("kvb", [128, 1024], BF16)
                rec = psb("rec", [128, 8], F32)
                lnk = psb("lnk", [128, 8], F32)
                ep10 = psb("ep10", [128, 1], F32)
                qtt = psb("qtt", [128, 2, 2048], BF16)
                atn = psb("atn", [128, 32, 2048], BF16)
                wow = psb("wow", [128, 2, DCH, 256], BF16)
                x1sl = psb("x1sl", [128, 2, 256], F32)
                x2sl = psb("x2sl", [128, 2, 256], F32)
                bo_s = psb("bo_s", [1, D], BF16)

                with nc.Block() as b:
                    @b.sync
                    def _(sy):
                        sy.wait_ge(sCC, 1)
                        sy.dma_start(out=kvr[:], in_=cc_dst_t[:, :]).then_inc(sKVl, 16)
                        sy.dma_start(out=kvr[:], in_=cc_dst_t[:, :]).then_inc(sKVl, 16)
                        for qh in range(32):
                            sy.wait_ge(sQW, 16 * (qh + 1))
                            if qh >= 2:
                                sy.wait_ge(sPEa, 4 * (qh - 1))
                            sy.dma_start(out=qtt[:, qh % 2, :], in_=q_scr[qh, :, :]).then_inc(sQr, 16)
                        sy.dma_start(out=qtt[:, 31 % 2, :], in_=q_scr[31, :, :]).then_inc(sQr, 16)
                        sy.wait_ge(sX1W, 16 * NTC)
                        for os in range(16):
                            for tc in range(NTC):
                                n = os * 16 + tc
                                if n >= 2:
                                    sy.wait_ge(sEvO, n - 1)
                                sy.dma_start(out=x1sl[:, n % 2, :],
                                             in_=x1_scr[tc * 128:(tc + 1) * 128, os * 256:(os + 1) * 256]).then_inc(sX1s, 16)
                                sy.dma_start(out=x1sl[:, n % 2, :],
                                             in_=x1_scr[tc * 128:(tc + 1) * 128, os * 256:(os + 1) * 256]).then_inc(sX1s, 16)
                        sy.dma_start(out=x1sl[:, 255 % 2, :], in_=x1_scr[15 * 128:16 * 128, 15 * 256:16 * 256]).then_inc(sX1s, 16)
                        sy.dma_start(out=x1sl[:, 255 % 2, :], in_=x1_scr[15 * 128:16 * 128, 15 * 256:16 * 256]).then_inc(sX1s, 16)

                    @b.gpsimd
                    def _(g):
                        g.dma_start(out=bo_s[:], in_=bo_b[:]).then_inc(sWo, 16)
                        for w in range(16):
                            if w >= 2:
                                g.wait_ge(sPEo, 16 * (w - 1))
                            for rep in range(2):
                                for piece in range(4):
                                    src = woT[piece * 1024:(piece + 1) * 1024, w * 256:(w + 1) * 256]
                                    g.dma_start(out=wow[:, w % 2, piece * 8:(piece + 1) * 8, :],
                                                in_=src.rearrange("(c p) o -> p c o", p=128)).then_inc(sWo, 16)
                            g.dma_start(out=wow[:, w % 2, 0, :],
                                        in_=woT[0:128, w * 256:(w + 1) * 256]).then_inc(sWo, 16)

                    @b.sync
                    def _(sy2):
                        if DBG4:
                            sy2.wait_ge(sKVl, 32)
                            sy2.dma_start(out=dbg_kvr[:, :], in_=kvr[:]).then_inc(sDbg3, 16)
                            sy2.wait_ge(x2sem, 16 * 256)
                            sy2.dma_start(out=dbg_atn[:, :, :], in_=atn[:, :, :]).then_inc(sDbg3, 16)
                            sy2.wait_ge(sDbg3, 32)

                    @b.vector
                    def _(v):
                        v.memset(ep10[:], 1e-10)
                        v.wait_ge(sKVl, 32)
                        v.tensor_copy(kvb[:], kvr[:, 0:1024])
                        v.tensor_copy(kvb[:, 992:1024], kvr[:, 992:1024]).then_inc(sKVb, 1)
                        for qh in range(32):
                            gh = qh // 4
                            for tb in range(4):
                                n = qh * 4 + tb
                                v.wait_ge(sPEa, n + 1)
                                if n == 0:
                                    v.wait_ge(sKVb, 2)
                                bank = pbank[n % 2]
                                rv = rec[:, gh:gh + 1]
                                rb = bass.AP(tensor=rv.tensor, offset=rv.offset,
                                             ap=[list(rv.ap[0]), [0, 512]])
                                v.scalar_tensor_tensor(atn[:, qh, tb * 512:(tb + 1) * 512], bank[:, 0:512],
                                                       1.0, rb, ALU.mult, ALU.mult)
                                rb32 = bass.AP(tensor=rv.tensor, offset=rv.offset,
                                               ap=[list(rv.ap[0]), [0, 32]])
                                v.scalar_tensor_tensor(atn[:, qh, tb * 512 + 480:(tb + 1) * 512],
                                                       bank[:, 480:512], 1.0, rb32,
                                                       ALU.mult, ALU.mult).then_inc(sEvA, 1)
                        for os in range(16):
                            for tc in range(NTC):
                                n = os * 16 + tc
                                v.wait_ge(sPEo, n + 1)
                                v.wait_ge(sX1s, min(32 * (n + 2), 32 * 256 + 32))
                                if n >= 2:
                                    v.wait_ge(x2sem, 16 * (n - 1))
                                bank = pbank[4 + n % 2]
                                v.scalar_tensor_tensor(x2sl[:, n % 2, :], bank[:, 0:256], 1.0,
                                                       x1sl[:, n % 2, :], ALU.mult, ALU.add)
                                v.scalar_tensor_tensor(x2sl[:, n % 2, 224:256], bank[:, 224:256], 1.0,
                                                       x1sl[:, n % 2, 224:256], ALU.mult, ALU.add).then_inc(sEvO, 1)

                    @b.tensor
                    def _(t):
                        t.wait_ge(sKVb, 1)
                        for qh in range(32):
                            gh = qh // 4
                            t.wait_ge(sQr, 16 * (qh + 2))
                            for tb in range(4):
                                n = qh * 4 + tb
                                bank = pbank[n % 2]
                                if n >= 2:
                                    t.wait_ge(sEvA, n - 1)
                                ins = t.matmul(bank[:, 0:512], kvb[:, gh * 128:(gh + 1) * 128],
                                               qtt[:, qh % 2, tb * 512:(tb + 1) * 512],
                                               start=True, stop=True)
                                ins.then_inc(sPEa, 1)
                        t.wait_ge(sEvA, 128)
                        for os in range(16):
                            t.wait_ge(sWo, 16 * (9 * (os + 1) + 1))
                            for tc in range(NTC):
                                n = os * 16 + tc
                                bank = pbank[4 + n % 2]
                                if n >= 2:
                                    t.wait_ge(sEvO, n - 1)
                                for vch in range(DCH):
                                    t.matmul(bank[:, 0:256],
                                             atn[:, vch, tc * 128:(tc + 1) * 128],
                                             wow[:, os % 2, vch, :],
                                             start=(vch == 0), stop=False)
                                ins = t.matmul(bank[:, 0:256], ones_r[:, :],
                                               bo_s[0:1, os * 256:(os + 1) * 256],
                                               start=False, stop=True)
                                ins.then_inc(sPEo, 1)

                    @b.scalar
                    def _(s):
                        s.wait_ge(sKVl, 32)
                        s.activation(lnk[:], kvr[:, 1024:1032], AF.Ln, bias=ep10[:, 0:1])
                        s.activation(rec[:], lnk[:], AF.Exp, scale=-1.0)
                        s.activation(rec[:], lnk[:], AF.Exp, scale=-1.0).then_inc(sKVb, 1)
                        for os in range(16):
                            for tc in range(NTC):
                                n = os * 16 + tc
                                s.wait_ge(sEvO, n + 1)
                                dst = x2dst[tc * 128:(tc + 1) * 128, os * 256:(os + 1) * 256]
                                s.dma_start(out=dst, in_=x2sl[:, n % 2, :]).then_inc(x2sem, 16)
                        s.wait_ge(x2sem, 16 * 256)

        if debug_stage == 4:
            emit_ffn_gu(1, x2_scr, 2)
            emit_ffn_down(1)
            emit_epilogue(1, x2_scr, out, None, sOutW)
            if DBG4:
                dbg_x2 = nc.dram_tensor("dbg_x2", [T, D], F32, kind="ExternalOutput").ap()
                dbg_xn = nc.dram_tensor("dbg_xn", [T, D], F32, kind="ExternalOutput").ap()
                dbg_y2 = nc.dram_tensor("dbg_y2", [T, D], F32, kind="ExternalOutput").ap()
                dbg_h4 = nc.dram_tensor("dbg_h4", [ICH, 128, T], F32, kind="ExternalOutput").ap()
                dbg_ng4 = nc.dram_tensor("dbg_ng4", [II, D], F32, kind="ExternalOutput").ap()
                dbg_nu4 = nc.dram_tensor("dbg_nu4", [II, D], F32, kind="ExternalOutput").ap()
                sDbg = S("sDbg")
                with nc.Block() as b:
                    @b.gpsimd
                    def _(g):
                        g.wait_ge(sOutW, 16 * NTC)
                        g.dma_start(out=dbg_x2[:, :], in_=x2_scr[:, :]).then_inc(sDbg, 16)
                        g.dma_start(out=dbg_xn[:, :], in_=xn_scr[:, :]).then_inc(sDbg, 16)
                        g.dma_start(out=dbg_y2[:, :], in_=y_scr[:, :]).then_inc(sDbg, 16)
                        g.dma_start(out=dbg_h4[:, :, :], in_=h_scr[:, :, :]).then_inc(sDbg, 16)
                        g.dma_start(out=dbg_ng4[:, :], in_=natg[:, :]).then_inc(sDbg, 16)
                        g.dma_start(out=dbg_nu4[:, :], in_=natu[:, :]).then_inc(sDbg, 16)
                        g.wait_ge(sDbg, 96)

    nc.finalize()
    return nc


def prep_inputs(inputs):
    sigma = np.concatenate([np.arange(0, II, 2), np.arange(1, II, 2)])
    f = {}
    f["wqT"] = np.ascontiguousarray(inputs["L0_attn_Wq"].T)
    f["wkT"] = np.ascontiguousarray(inputs["L0_attn_Wk"].T)
    f["wvT"] = np.ascontiguousarray(inputs["L0_attn_Wv"].T)
    f["woT"] = np.ascontiguousarray(inputs["L0_attn_Wo"].T)
    f["bq_pp"] = np.ascontiguousarray(inputs["L0_attn_bq"].reshape(32, 128).T)
    f["bkv_b"] = np.ascontiguousarray(np.concatenate([inputs["L0_attn_bk"], inputs["L0_attn_bv"]]).reshape(1, 2048))
    f["bo_b"] = np.ascontiguousarray(inputs["L0_attn_bo"].reshape(1, D))
    u8 = lambda a: np.ascontiguousarray(a.view(np.uint8)[:, 0::4])
    f["gq_u8"] = np.ascontiguousarray(u8(inputs["L0_ffn_gate_Wq"])[sigma])
    f["uq_u8"] = np.ascontiguousarray(u8(inputs["L0_ffn_up_Wq"])[sigma])
    f["dq_u8"] = u8(inputs["L0_ffn_down_Wq"])
    f["g_sc"] = np.ascontiguousarray(inputs["L0_ffn_gate_scale"][sigma])
    f["g_zp"] = np.ascontiguousarray(inputs["L0_ffn_gate_zero"][sigma])
    f["u_sc"] = np.ascontiguousarray(inputs["L0_ffn_up_scale"][sigma])
    f["u_zp"] = np.ascontiguousarray(inputs["L0_ffn_up_zero"][sigma])
    f["d_sc"] = np.ascontiguousarray(inputs["L0_ffn_down_scale"])
    f["d_zp"] = np.ascontiguousarray(inputs["L0_ffn_down_zero"])
    x = np.asarray(inputs["L0_attn_hidden_states"])
    shards = []
    for c in range(8):
        m = dict(f)
        m["x_in"] = np.ascontiguousarray(x[c // 2, (c % 2) * T:(c % 2 + 1) * T, :])
        shards.append(m)
    return shards


def kernel(**inputs):
    nc = build_program(4)
    shards = prep_inputs(inputs)
    res = run_bass_kernel_spmd(nc, shards, core_ids=list(range(8)))
    outs = [res.results[c]["out"] for c in range(8)]
    full = np.empty((4, 4096, D), np.float32)
    for c in range(8):
        full[c // 2, (c % 2) * T:(c % 2 + 1) * T, :] = outs[c]
    return full
